# version 07 (final)
# baseline (speedup 1.0000x reference)
import numpy as np
import concourse.bass as bass
import concourse.mybir as mybir
from concourse.bass_utils import run_bass_kernel_spmd

F32 = mybir.dt.float32
BF16 = mybir.dt.bfloat16
AX = mybir.AxisListType.X
ALU = mybir.AluOpType
ACT = mybir.ActivationFunctionType

MAGIC = 12582912.0
R = 4096
D = 1024
I3 = 3072
NT = 32
DC = 8
IC = 24
NCORES = 8
H = 16
QMAX = 32767.0
INV7 = float(np.float32(1.0) / np.float32(7.0))
INVQ = float(np.float32(1.0) / np.float32(QMAX))
INVQ16 = float(np.float32(2.0 ** -16) * np.float32(INVQ))

XN = 0
PBUF = 0
SQS = 2048
SFA = 4096
SFB = 12288
SVV = 20480
SVK = 24576
ATT = 28672
WREG = 32768


class Sem:
    def __init__(self, handle):
        self.h = handle
        self.n = 0


class Prog:
    ENGINES = ("sync", "gpsimd", "vector", "scalar", "tensor")

    def __init__(self):
        self.q = {e: [] for e in self.ENGINES}

    def emit(self, eng, fn):
        self.q[eng].append(fn)

    def wait(self, eng, sem, val):
        if val > 0:
            self.q[eng].append(lambda e, s=sem.h, v=val: e.wait_ge(s, v))


def build_program():
    nc = bass.Bass("TRN2", target_bir_lowering=False, debug=False)

    x_in = nc.dram_tensor("x_in", [R, D], F32, kind="ExternalInput").ap()
    wq_in = nc.dram_tensor("wq_in", [D, D], F32, kind="ExternalInput").ap()
    wk_in = nc.dram_tensor("wk_in", [D, D], F32, kind="ExternalInput").ap()
    wv_in = nc.dram_tensor("wv_in", [D, D], F32, kind="ExternalInput").ap()
    wo_in = nc.dram_tensor("wo_in", [D, D], F32, kind="ExternalInput").ap()
    betas_in = nc.dram_tensor("betas_in", [1, 1024], F32, kind="ExternalInput").ap()
    m0_in = nc.dram_tensor("m0_in", [1024, 64], F32, kind="ExternalInput").ap()
    z0_in = nc.dram_tensor("z0_in", [1, 1024], F32, kind="ExternalInput").ap()
    wg_in = nc.dram_tensor("wg_in", [I3, D], F32, kind="ExternalInput").ap()
    wu_in = nc.dram_tensor("wu_in", [I3, D], F32, kind="ExternalInput").ap()
    wd_in = nc.dram_tensor("wd_in", [D, I3], F32, kind="ExternalInput").ap()

    out_d = nc.dram_tensor("out", [R, D], F32, kind="ExternalOutput").ap()
    x1_d = nc.dram_tensor("x1_dbg", [R, D], F32, kind="ExternalOutput").ap()
    x2_d = nc.dram_tensor("x2_dbg", [R, D], F32, kind="ExternalOutput").ap()

    wu_q = nc.dram_tensor("wu_q", [I3, D], BF16, kind="Internal").ap()
    wg_q = nc.dram_tensor("wg_q", [I3, D], BF16, kind="Internal").ap()
    wd_q = nc.dram_tensor("wd_q", [D, I3], BF16, kind="Internal").ap()
    up_sp = nc.dram_tensor("up_sp", [I3, R], F32, kind="Internal").ap()
    gate_sp = nc.dram_tensor("gate_sp", [I3, R], F32, kind="Internal").ap()
    g2_sp = nc.dram_tensor("g2_sp", [I3, R], F32, kind="Internal").ap()
    oo_sp = nc.dram_tensor("oo_sp", [I3, R], F32, kind="Internal").ap()
    gsig_d = nc.dram_tensor("gsig_d", [1, 1024], F32, kind="Internal").ap()
    pr_d = nc.dram_tensor("pr_d", [1, 128], F32, kind="Internal").ap()
    lrow_d = nc.dram_tensor("lrow_d", [1, 512], F32, kind="Internal").ap()
    cc_ins = [nc.dram_tensor(f"cc_in{k}", [1, 2], F32, kind="Internal").ap() for k in range(8)]
    cc_outs = [nc.dram_tensor(f"cc_out{k}", [1, 2], F32, kind="Internal").ap() for k in range(8)]

    from contextlib import ExitStack
    es = ExitStack()
    with es:
        E = es.enter_context
        abf = E(nc.sbuf_tensor("abf", [128, 65536], BF16))
        af = E(nc.sbuf_tensor("af", [128, 12288], F32))
        io512 = E(nc.sbuf_tensor("io512", [128, 512], F32))
        eye_bf = E(nc.sbuf_tensor("eye_bf", [128, 128], BF16))
        eye8_bf = E(nc.sbuf_tensor("eye8_bf", [128, 128], BF16))
        masks = E(nc.sbuf_tensor("masks", [128, 2048], BF16))
        lcol = E(nc.sbuf_tensor("lcol", [128, 4], F32))
        ones_bf = E(nc.sbuf_tensor("ones_bf", [128, 1], BF16))
        onesrow_f = E(nc.sbuf_tensor("onesrow_f", [1, 128], F32))
        m_all = E(nc.sbuf_tensor("m_all", [128, 32], F32))
        rs_all = E(nc.sbuf_tensor("rs_all", [128, 32], F32))
        rmax = E(nc.sbuf_tensor("rmax", [128, 8], F32))
        scl = E(nc.sbuf_tensor("scl", [128, 16], F32))
        srow = E(nc.sbuf_tensor("srow", [1, 16], F32))
        Mz = E(nc.sbuf_tensor("Mz", [64, 2080], F32))
        Mx_bf = E(nc.sbuf_tensor("Mx_bf", [64, 2080], BF16))
        gate_c = E(nc.sbuf_tensor("gate_c", [64, 16], F32))
        q2b = E(nc.sbuf_tensor("q2b", [128, 512], BF16))
        cseg = E(nc.sbuf_tensor("cseg", [128, 1536], F32))
        rd_row = cseg[0:1, 1024:1536]
        rz_row = cseg[0:1, 0:512]
        negc = cseg[0:1, 512:1024]
        eps_c = E(nc.sbuf_tensor("eps_c", [128, 1], F32))
        c0125 = E(nc.sbuf_tensor("c0125", [128, 1], F32))
        c65536 = E(nc.sbuf_tensor("c65536", [128, 1], F32))
        xqt_bf = E(nc.sbuf_tensor("xqt_bf", [128, 2048], BF16))
        ccs = E(nc.sbuf_tensor("ccs", [1, 2], F32))
        prr = E(nc.sbuf_tensor("prr", [1, 256], F32))

        ps_f = E(nc.psum_tensor("ps_f", [128, 3072], F32))

        qS = Sem(E(nc.semaphore("qS")))
        qG = Sem(E(nc.semaphore("qG")))
        sv = Sem(E(nc.semaphore("sv")))
        ss = Sem(E(nc.semaphore("ss")))
        st = Sem(E(nc.semaphore("st")))
        sg = Sem(E(nc.semaphore("sg")))

        P = Prog()

        def dmaS(dst, src, transpose=False):
            qS.n += 16
            if transpose:
                P.emit("sync", lambda e, d=dst, s=src: e.dma_start(d, s, transpose=True).then_inc(qS.h, 16))
            else:
                P.emit("sync", lambda e, d=dst, s=src: e.dma_start(d, s).then_inc(qS.h, 16))
            return qS.n

        def dmaG(dst, src):
            qG.n += 16
            P.emit("gpsimd", lambda e, d=dst, s=src: e.dma_start(d, s).then_inc(qG.h, 16))
            return qG.n

        def v(fn):
            P.emit("vector", fn)

        def s_(fn):
            P.emit("scalar", fn)

        def vmark():
            P.emit("vector", lambda e: e.drain())
            sv.n += 1
            P.emit("vector", lambda e: e.sem_inc(sv.h, 1))
            return sv.n

        def smark():
            P.emit("scalar", lambda e: e.drain())
            ss.n += 1
            P.emit("scalar", lambda e: e.sem_inc(ss.h, 1))
            return ss.n

        def tmm(fn):
            st.n += 1
            P.emit("tensor", lambda e, f=fn: f(e).then_inc(st.h, 1))
            return st.n

        def top(fn):
            P.emit("tensor", fn)

        def gmark():
            sg.n += 1
            P.emit("gpsimd", lambda e: e.sem_inc(sg.h, 1))
            return sg.n

        def barrier():
            P.wait("sync", qS, qS.n)
            P.wait("gpsimd", qG, qG.n)
            base = sg.n
            for eng in P.ENGINES:
                sg.n += 1
                P.emit(eng, lambda e: e.sem_inc(sg.h, 1))
            tgt = base + 5
            for eng in P.ENGINES:
                P.wait(eng, sg, tgt)

        def slot(i, w=1024):
            return af[:, i * 1024:i * 1024 + w]

        def xq_fm(dc):
            return abf[:, dc * 4096:(dc + 1) * 4096]

        def wbig(c0, w):
            return abf[:, WREG + c0: WREG + c0 + w]

        P.emit("gpsimd", lambda e: e.iota(io512[:], [[1, 512]], channel_multiplier=-1,
                                          allow_small_or_imprecise_dtypes=True))
        P.emit("gpsimd", lambda e: e.iota(
            af[0:1, 0:512].rearrange("p (s t) -> p s t", t=32),
            [[1, 16], [16, 32]], channel_multiplier=0,
            allow_small_or_imprecise_dtypes=True))
        g_io = gmark()
        P.wait("vector", sg, g_io)
        v(lambda e: e.tensor_scalar(eye_bf[:], io512[:, 0:128], 0.0, None, ALU.is_equal))
        v(lambda e: e.tensor_scalar(eye8_bf[:], io512[:, 0:128], 0.0, 8.0,
                                    ALU.is_equal, ALU.mult))
        v(lambda e: e.memset(ones_bf[:], 1.0))
        v(lambda e: e.memset(onesrow_f[:], 1.0))
        v(lambda e: e.memset(eps_c[:], 1e-5))
        v(lambda e: e.memset(c0125[:], 0.125))
        v(lambda e: e.memset(c65536[:], 65536.0))
        v_const = vmark()
        P.wait("tensor", sv, v_const)
        P.wait("tensor", sg, g_io)
        P.wait("scalar", sv, v_const)
        stL = dmaG(lrow_d, af[0:1, 0:512])
        P.wait("sync", qG, stL)
        for jc in range(4):
            ldL = dmaS(lcol[:, jc:jc + 1],
                       lrow_d[0:1, jc * 128:(jc + 1) * 128].rearrange("o (k u) -> (o k) u", u=1))
        tB = tmm(lambda e: e.matmul(ps_f[:, 0:512], onesrow_f[0:1, 0:128],
                                    af[0:1, 0:512], start=True, stop=True,
                                    skip_group_check=True))
        P.wait("vector", st, tB)
        P.wait("vector", qS, ldL)
        for jc in range(4):
            v(lambda e, jc=jc: e.tensor_scalar(
                masks[:, jc * 512:(jc + 1) * 512], ps_f[:, 0:512],
                lcol[:, jc:jc + 1], None, ALU.is_ge))
        v_mask = vmark()
        P.wait("sync", sv, v_mask)
        P.wait("tensor", sv, v_mask)

        def dequant_weight(w_in_ap, w_q_ap, ntiles, width, nblk):
            P.wait("sync", sv, sv.n)
            vmarks = {}
            stmarks = {}
            for it in range(ntiles):
                buf = it % 2
                wt = af[:, buf * 3072: buf * 3072 + width]
                if it >= 2:
                    P.wait("sync", sv, vmarks[it - 2])
                ld = dmaS(wt, w_in_ap[it * 128:(it + 1) * 128, :])
                P.wait("vector", qS, ld)
                bm = af[:, 6144:6144 + nblk]
                sw = af[:, 6784:6784 + nblk]
                rw = af[:, 7424:7424 + nblk]
                y = af[:, 8192:8192 + width]
                v(lambda e, wt=wt, bm=bm, nblk=nblk: e.tensor_reduce(
                    bm.rearrange("p (b o) -> p b o", o=1),
                    wt.rearrange("p (b k) -> p b k", k=32),
                    AX, ALU.max, apply_absolute_value=True))
                v(lambda e: e.drain())
                v(lambda e, bm=bm, sw=sw: e.tensor_scalar(sw, bm, INV7, 1e-12, ALU.mult, ALU.max))
                v(lambda e: e.drain())
                v(lambda e, sw=sw, rw=rw: e.reciprocal(rw, sw))
                v(lambda e: e.drain())
                v(lambda e, wt=wt, rw=rw, y=y, nblk=nblk: e.tensor_tensor(
                    y.rearrange("p (b k) -> p b k", k=32),
                    wt.rearrange("p (b k) -> p b k", k=32),
                    rw.rearrange("p (b o) -> p b o", o=1).broadcast_to((128, nblk, 32)),
                    ALU.mult))
                v(lambda e, y=y: e.tensor_scalar(y, y, MAGIC, -MAGIC, ALU.add, ALU.add))
                v(lambda e: e.drain())
                dst_bf = abf[:, buf * 3072: buf * 3072 + width]
                if it >= 2:
                    P.wait("vector", qG, stmarks[it - 2])
                v(lambda e, y=y, sw=sw, dst=dst_bf, nblk=nblk: e.tensor_tensor(
                    dst.rearrange("p (b k) -> p b k", k=32),
                    y.rearrange("p (b k) -> p b k", k=32),
                    sw.rearrange("p (b o) -> p b o", o=1).broadcast_to((128, nblk, 32)),
                    ALU.mult))
                vmarks[it] = vmark()
                P.wait("gpsimd", sv, vmarks[it])
                stmarks[it] = dmaG(w_q_ap[it * 128:(it + 1) * 128, :], dst_bf)

        dequant_weight(wu_in, wu_q, IC, D, 32)
        dequant_weight(wg_in, wg_q, IC, D, 32)
        dequant_weight(wd_in, wd_q, DC, I3, 96)
        barrier()

        def ln_stats(src_ap, with_absmax, rmax_col):
            if with_absmax:
                v(lambda e, c=rmax_col: e.memset(rmax[:, c:c + 1], 0.0))
            consume = {}
            for t in range(NT):
                buf = t % 2
                xt = slot(buf)
                if t >= 2:
                    P.wait("sync", sv, consume[t - 2])
                ld = dmaS(xt, src_ap[t * 128:(t + 1) * 128, :])
                P.wait("vector", qS, ld)
                mcol = m_all[:, t:t + 1]
                rcol = rs_all[:, t:t + 1]
                v(lambda e, xt=xt, m=mcol: e.reduce_sum(m, xt, AX))
                v(lambda e: e.drain())
                v(lambda e, m=mcol: e.tensor_scalar(m, m, 1.0 / 1024.0, None, ALU.mult))
                v(lambda e: e.drain())
                d_ = slot(2)
                v(lambda e, xt=xt, m=mcol, d=d_: e.tensor_scalar(d, xt, m, None, ALU.subtract))
                consume[t] = vmark()
                t2 = slot(3)
                v(lambda e, d=d_, t2=t2: e.tensor_tensor(t2, d, d, ALU.mult))
                v(lambda e: e.drain())
                vcol = scl[:, 15:16]
                v(lambda e, t2=t2, vc=vcol: e.reduce_sum(vc, t2, AX))
                v(lambda e: e.drain())
                v(lambda e, vc=vcol: e.tensor_scalar(vc, vc, 1.0 / 1024.0, None, ALU.mult))
                vv = vmark()
                P.wait("scalar", sv, vv)
                sd = s_(lambda e, vc=vcol: e.activation(vc, vc, ACT.Sqrt, bias=eps_c[:, 0:1]))
                sd = smark()
                P.wait("vector", ss, sd)
                v(lambda e, vc=vcol, r=rcol: e.reciprocal(r, vc))
                v(lambda e: e.drain())
                if with_absmax:
                    acol = scl[:, 14:15]
                    v(lambda e, d=d_, a=acol: e.tensor_reduce(a, d, AX, ALU.max,
                                                              apply_absolute_value=True))
                    v(lambda e: e.drain())
                    v(lambda e, a=acol, r=rcol: e.tensor_tensor(a, a, r, ALU.mult))
                    v(lambda e: e.drain())
                    v(lambda e, a=acol, c=rmax_col: e.tensor_tensor(
                        rmax[:, c:c + 1], rmax[:, c:c + 1], a, ALU.max))
                    v(lambda e: e.drain())

        def part_reduce_maxcols(cols):
            vv = vmark()
            P.wait("sync", sv, vv)
            last = 0
            for i, c in enumerate(cols):
                last = dmaS(prr[0:1, i * 128:(i + 1) * 128], rmax[:, c:c + 1])
            P.wait("vector", qS, last)
            for i in range(len(cols)):
                v(lambda e, i=i: e.reduce_max(srow[0:1, i:i + 1],
                                              prr[0:1, i * 128:(i + 1) * 128], AX))
                v(lambda e: e.drain())

        def allreduce_scales(bar_idx, n_vals, qdiv):
            n = n_vals
            v(lambda e, n=n: e.tensor_copy(ccs[0:1, 0:n], srow[0:1, 0:n]))
            vv = vmark()
            P.wait("gpsimd", sv, vv)
            stv = dmaG(cc_ins[bar_idx][0:1, 0:n], ccs[0:1, 0:n])
            P.wait("gpsimd", qG, stv)
            sg.n += 1
            P.emit("gpsimd", lambda e, k=bar_idx: e.collective_compute(
                "AllReduce", ALU.max, replica_groups=[[0, 1, 2, 3, 4, 5, 6, 7]],
                ins=[cc_ins[k]], outs=[cc_outs[k]]).then_inc(sg.h, 1))
            sg.n += 1
            P.emit("gpsimd", lambda e: e.sem_inc(sg.h, 1))
            cdone = sg.n
            P.wait("sync", sg, cdone)
            ldv = dmaS(ccs[0:1, 0:n], cc_outs[bar_idx][0:1, 0:n])
            P.wait("vector", qS, ldv)
            v(lambda e, n=n, q=qdiv: e.tensor_scalar(
                srow[0:1, 8:8 + n], ccs[0:1, 0:n], q, 1e-12, ALU.mult, ALU.max))
            v(lambda e: e.drain())
            v(lambda e, n=n: e.reciprocal(srow[0:1, 8 + n:8 + 2 * n], srow[0:1, 8:8 + n]))
            vv = vmark()
            P.wait("tensor", sv, vv)
            td = tmm(lambda e, n=n: e.matmul(ps_f[:, 2560:2560 + 2 * n], onesrow_f[:],
                                             srow[0:1, 8:8 + 2 * n], start=True, stop=True,
                                             skip_group_check=True))
            P.wait("vector", st, td)
            v(lambda e, n=n: e.tensor_copy(scl[:, 0:2 * n], ps_f[:, 2560:2560 + 2 * n]))
            return vmark()

        def ffn(src_ap, dst_ap, bar0):
            ln_stats(src_ap, True, 0)
            part_reduce_maxcols([0])
            allreduce_scales(bar0, 1, INVQ)
            barrier()

            xbguard = {}
            tpm = {}
            for t in range(NT):
                buf = t % 2
                xt = slot(buf)
                if t >= 2:
                    P.wait("sync", sv, xbguard[t - 2])
                ld = dmaS(xt, src_ap[t * 128:(t + 1) * 128, :])
                P.wait("vector", qS, ld)
                d_ = slot(2)
                v(lambda e, xt=xt, t=t, d=d_: e.tensor_scalar(
                    d, xt, m_all[:, t:t + 1], None, ALU.subtract))
                y = slot(3)
                v(lambda e, d=d_, t=t, y=y: e.tensor_scalar(
                    y, d, rs_all[:, t:t + 1], scl[:, 1:2], ALU.mult, ALU.mult))
                v(lambda e, y=y: e.tensor_scalar(y, y, MAGIC, -MAGIC, ALU.add, ALU.add))
                v(lambda e: e.drain())
                xb = xqt_bf[:, buf * 1024: buf * 1024 + 1024]
                if t >= 2:
                    P.wait("vector", qS, tpm[t - 2])
                v(lambda e, y=y, xb=xb: e.tensor_scalar(xb, y, scl[:, 0:1], None, ALU.mult))
                xbguard[t] = vmark()
                P.wait("sync", sv, xbguard[t])
                dstv = abf[:, 0:32768].rearrange("p (m q) -> p m q", q=4096)[
                    :, :, t * 128:t * 128 + 128]
                tpm[t] = dmaS(dstv, xb, transpose=True)
            barrier()

            for wi, (w_q_ap, sp_ap, rmc) in enumerate(((wu_q, up_sp, 1), (wg_q, gate_sp, 2))):
                P.wait("sync", st, st.n)
                lds = []
                for dc in range(DC):
                    lds.append(dmaS(wbig(dc * 3072, 3072), w_q_ap[:, dc * 128:(dc + 1) * 128],
                                    transpose=True))
                v(lambda e, c=rmc: e.memset(rmax[:, c:c + 1], 0.0))
                vmark()
                P.wait("tensor", qS, lds[-1])
                evs = []
                reds = []
                sts = []
                nuse = 0
                for ic in range(IC):
                    for q4 in range(4):
                        sl = nuse % 3
                        pbase = sl * 1024
                        if nuse >= 3:
                            P.wait("tensor", ss, evs[nuse - 3])
                        for dc in range(DC):
                            first, last = dc == 0, dc == DC - 1
                            top(lambda e, ic=ic, q4=q4, dc=dc, pbase=pbase, first=first, last=last:
                                e.matmul(ps_f[:, pbase:pbase + 512],
                                         wbig(dc * 3072 + ic * 128, 128),
                                         xq_fm(dc)[:, q4 * 1024: q4 * 1024 + 512],
                                         start=first, stop=last, skip_group_check=True))
                            fn2 = (lambda e, ic=ic, q4=q4, dc=dc, pbase=pbase, first=first, last=last:
                                   e.matmul(ps_f[:, pbase + 512:pbase + 1024],
                                            wbig(dc * 3072 + ic * 128, 128),
                                            xq_fm(dc)[:, q4 * 1024 + 512: q4 * 1024 + 1024],
                                            start=first, stop=last, skip_group_check=True))
                            if last:
                                td = tmm(fn2)
                            else:
                                top(fn2)
                        sslot = 4 + sl
                        P.wait("scalar", st, td)
                        if nuse >= 3:
                            P.wait("scalar", sv, reds[nuse - 3])
                            P.wait("scalar", qG, sts[nuse - 3])
                        s_(lambda e, pbase=pbase, sl2=sslot: e.activation(
                            slot(sl2), ps_f[:, pbase:pbase + 1024], ACT.Copy))
                        ev = smark()
                        evs.append(ev)
                        P.wait("vector", ss, ev)
                        v(lambda e, sl2=sslot: e.tensor_reduce(
                            scl[:, 14:15], slot(sl2), AX, ALU.max, apply_absolute_value=True))
                        v(lambda e: e.drain())
                        v(lambda e, c=rmc: e.tensor_tensor(
                            rmax[:, c:c + 1], rmax[:, c:c + 1], scl[:, 14:15], ALU.max))
                        reds.append(vmark())
                        P.wait("gpsimd", ss, ev)
                        sts.append(dmaG(sp_ap[ic * 128:(ic + 1) * 128, q4 * 1024:(q4 + 1) * 1024],
                                        slot(sslot)))
                        nuse += 1
            barrier()
            part_reduce_maxcols([1, 2])
            allreduce_scales(bar0 + 1, 2, INVQ)
            v(lambda e: e.tensor_copy(scl[:, 8:9], scl[:, 0:1]))
            v(lambda e: e.tensor_copy(scl[:, 9:10], scl[:, 2:3]))
            vmark()
            barrier()

            v(lambda e: e.memset(rmax[:, 3:4], 0.0))
            vmark()
            smG = {}
            svG = {}
            for ic in range(IC):
                for hh in range(2):
                    i2 = ic * 2 + hh
                    buf = i2 % 2
                    gb = af[:, buf * 2048: buf * 2048 + 2048]
                    if i2 >= 2:
                        P.wait("sync", ss, smG[i2 - 2])
                    ld = dmaS(gb, gate_sp[ic * 128:(ic + 1) * 128, hh * 2048:(hh + 1) * 2048])
                    P.wait("scalar", qS, ld)
                    y = af[:, 4096:6144] if buf == 0 else af[:, 10240:12288]
                    if i2 >= 1:
                        P.wait("scalar", sv, svG[i2 - 1])
                    if i2 >= 2:
                        P.wait("scalar", qG, svG[("st", i2 - 2)])
                    s_(lambda e, gb=gb, y=y: e.activation(y, gb, ACT.Copy, scale=scl[:, 3:4]))
                    s1 = smark()
                    smG[i2] = s1
                    P.wait("vector", ss, s1)
                    Rn = af[:, 6144:8192]
                    v(lambda e, y=y, R_=Rn: e.tensor_scalar(R_, y, MAGIC, -MAGIC, ALU.add, ALU.add))
                    v1 = vmark()
                    P.wait("scalar", sv, v1)
                    sig = af[:, 8192:10240]
                    s_(lambda e, R_=Rn, sg2=sig: e.activation(sg2, R_, ACT.Sigmoid, scale=scl[:, 1:2]))
                    s_(lambda e, sg2=sig: e.activation(sg2, sg2, ACT.Copy, scale=c65536[:, 0:1]))
                    s3 = smark()
                    P.wait("vector", ss, s3)
                    v(lambda e, u=sig: e.tensor_scalar(u, u, MAGIC, -MAGIC, ALU.add, ALU.add))
                    v(lambda e: e.drain())
                    v(lambda e, u=sig: e.tensor_scalar(u, u, 65535.0, None, ALU.min))
                    v(lambda e: e.drain())
                    G = y
                    v(lambda e, R_=Rn, u=sig, G=G: e.scalar_tensor_tensor(
                        G, R_, scl[:, 1:2], u, ALU.mult, ALU.mult))
                    v(lambda e: e.drain())
                    v(lambda e, G=G: e.tensor_reduce(scl[:, 14:15], G, AX, ALU.max,
                                                     apply_absolute_value=True))
                    v(lambda e: e.drain())
                    v(lambda e: e.tensor_tensor(rmax[:, 3:4], rmax[:, 3:4], scl[:, 14:15], ALU.max))
                    vd = vmark()
                    svG[i2] = vd
                    P.wait("gpsimd", sv, vd)
                    svG[("st", i2)] = dmaG(g2_sp[ic * 128:(ic + 1) * 128,
                                                 hh * 2048:(hh + 1) * 2048], G)
            barrier()
            part_reduce_maxcols([3])
            allreduce_scales(bar0 + 2, 1, INVQ16)
            v(lambda e: e.tensor_scalar(scl[:, 2:3], scl[:, 1:2], float(2.0 ** -16), None, ALU.mult))
            vmark()
            barrier()

            v(lambda e: e.memset(rmax[:, 4:5], 0.0))
            vmark()
            smO = {}
            svO = {}
            for ic in range(IC):
                for hh in range(2):
                    i2 = ic * 2 + hh
                    buf = i2 % 2
                    gb = af[:, buf * 2048: buf * 2048 + 2048]
                    ub = af[:, 4096 + buf * 2048: 4096 + buf * 2048 + 2048]
                    if i2 >= 2:
                        P.wait("sync", sv, svO[i2 - 2])
                        P.wait("sync", qG, svO[("st", i2 - 2)])
                    dmaS(gb, g2_sp[ic * 128:(ic + 1) * 128, hh * 2048:(hh + 1) * 2048])
                    ld = dmaS(ub, up_sp[ic * 128:(ic + 1) * 128, hh * 2048:(hh + 1) * 2048])
                    P.wait("scalar", qS, ld)
                    yg = af[:, 8192:10240]
                    yu = af[:, 10240:12288]
                    if i2 >= 1:
                        P.wait("scalar", sv, smO.get(("v", i2 - 1), 0))
                    s_(lambda e, gb=gb, y=yg: e.activation(y, gb, ACT.Copy, scale=scl[:, 2:3]))
                    s_(lambda e, ub=ub, y=yu: e.activation(y, ub, ACT.Copy, scale=scl[:, 9:10]))
                    s2 = smark()
                    smO[i2] = s2
                    P.wait("vector", ss, s2)
                    v(lambda e, y=yg: e.tensor_scalar(y, y, MAGIC, -MAGIC, ALU.add, ALU.add))
                    v(lambda e, y=yu: e.tensor_scalar(y, y, MAGIC, -MAGIC, ALU.add, ALU.add))
                    v(lambda e: e.drain())
                    v(lambda e, y=yu: e.tensor_scalar(y, y, scl[:, 8:9], None, ALU.mult))
                    v(lambda e: e.drain())
                    if i2 >= 2:
                        P.wait("vector", qG, svO.get(("st", i2 - 2), 0))
                    v(lambda e, yg=yg, yu=yu, dst=gb: e.scalar_tensor_tensor(
                        dst, yg, scl[:, 0:1], yu, ALU.mult, ALU.mult))
                    v(lambda e: e.drain())
                    v(lambda e, dst=gb: e.tensor_reduce(scl[:, 14:15], dst, AX, ALU.max,
                                                        apply_absolute_value=True))
                    v(lambda e: e.drain())
                    v(lambda e: e.tensor_tensor(rmax[:, 4:5], rmax[:, 4:5], scl[:, 14:15], ALU.max))
                    vd = vmark()
                    svO[i2] = vd
                    smO[("v", i2)] = vd
                    P.wait("gpsimd", sv, vd)
                    svO[("st", i2)] = dmaG(oo_sp[ic * 128:(ic + 1) * 128,
                                                 hh * 2048:(hh + 1) * 2048], gb)
            barrier()
            part_reduce_maxcols([4])
            allreduce_scales(bar0 + 3, 1, INVQ)
            barrier()

            lds = []
            for ic in range(IC):
                lds.append(dmaS(wbig(ic * 1024, 1024), wd_q[:, ic * 128:(ic + 1) * 128],
                                transpose=True))
            ev2 = []
            ev2st = []
            last_grp_mm = 0
            for grp in range(4):
                if grp > 0:
                    P.wait("vector", st, last_grp_mm)
                prepm = {}
                for ic in range(IC):
                    buf = ic % 2
                    ob = af[:, buf * 1024: buf * 1024 + 1024]
                    if ic >= 2:
                        P.wait("sync", ss, prepm[ic - 2])
                    ld = dmaS(ob, oo_sp[ic * 128:(ic + 1) * 128, grp * 1024:(grp + 1) * 1024])
                    P.wait("scalar", qS, ld)
                    y = af[:, 2048 + buf * 1024: 2048 + buf * 1024 + 1024]
                    if ic >= 1:
                        P.wait("scalar", sv, prepm.get(("v", ic - 1), 0))
                    s_(lambda e, ob=ob, y=y: e.activation(y, ob, ACT.Copy, scale=scl[:, 1:2]))
                    s1 = smark()
                    prepm[ic] = s1
                    P.wait("vector", ss, s1)
                    v(lambda e, y=y: e.tensor_scalar(y, y, MAGIC, -MAGIC, ALU.add, ALU.add))
                    v(lambda e: e.drain())
                    v(lambda e, y=y, ic=ic: e.tensor_scalar(
                        abf[:, ic * 1024:(ic + 1) * 1024], y, scl[:, 0:1], None, ALU.mult))
                    prepm[("v", ic)] = vmark()
                if grp == 0:
                    P.wait("tensor", qS, lds[-1])
                P.wait("tensor", sv, sv.n)
                for tt in range(8):
                    t = grp * 8 + tt
                    pbase = (tt % 2) * 1024
                    if len(ev2) >= 2:
                        P.wait("tensor", sv, ev2[-2])
                    for ic in range(IC):
                        first, last = ic == 0, ic == IC - 1
                        top(lambda e, ic=ic, tt=tt, pbase=pbase, first=first, last=last:
                            e.matmul(ps_f[:, pbase:pbase + 512],
                                     abf[:, ic * 1024 + tt * 128: ic * 1024 + tt * 128 + 128],
                                     wbig(ic * 1024, 512),
                                     start=first, stop=last, skip_group_check=True))
                        fn2 = (lambda e, ic=ic, tt=tt, pbase=pbase, first=first, last=last:
                               e.matmul(ps_f[:, pbase + 512:pbase + 1024],
                                        abf[:, ic * 1024 + tt * 128: ic * 1024 + tt * 128 + 128],
                                        wbig(ic * 1024 + 512, 512),
                                        start=first, stop=last, skip_group_check=True))
                        if last:
                            td = tmm(fn2)
                        else:
                            top(fn2)
                    if tt == 7:
                        last_grp_mm = td
                    xt = af[:, 4096 + (tt % 2) * 1024: 4096 + (tt % 2) * 1024 + 1024]
                    if len(ev2) >= 2:
                        P.wait("sync", sv, ev2[-2])
                    ld = dmaS(xt, src_ap[t * 128:(t + 1) * 128, :])
                    P.wait("vector", qS, ld)
                    P.wait("vector", st, td)
                    ot = af[:, 6144 + (tt % 2) * 1024: 6144 + (tt % 2) * 1024 + 1024]
                    if len(ev2) >= 2:
                        P.wait("vector", qG, ev2st[-2])
                    v(lambda e, pbase=pbase, xt=xt, ot=ot: e.scalar_tensor_tensor(
                        ot, ps_f[:, pbase:pbase + 1024], 0.5, xt, ALU.mult, ALU.add))
                    vd = vmark()
                    ev2.append(vd)
                    P.wait("gpsimd", sv, vd)
                    ev2st.append(dmaG(dst_ap[t * 128:(t + 1) * 128, :], ot))
            barrier()

        ffn(x_in, x1_d, 0)

        ln_stats(x1_d, False, 0)
        for wi, w_ap in enumerate((wq_in, wk_in, wv_in, wo_in)):
            for dc in range(DC):
                dmaG(wbig(wi * 8192 + dc * 1024, 1024), w_ap[dc * 128:(dc + 1) * 128, :])
        aw_done = qG.n
        ldb = dmaS(af[0:1, 0:1024], betas_in)
        P.wait("scalar", qS, ldb)
        s_(lambda e: e.activation(af[0:1, 1024:2048], af[0:1, 0:1024], ACT.Sigmoid))
        sd = smark()
        P.wait("gpsimd", ss, sd)
        stv = dmaG(gsig_d, af[0:1, 1024:2048])
        P.wait("sync", qG, stv)
        for h in range(H):
            dmaS(gate_c[0:64, h:h + 1],
                 gsig_d[0:1, h * 64:(h + 1) * 64].rearrange("o (k u) -> (o k) u", u=1))
        for b in range(2):
            for h in range(H):
                mb = (b * 16 + h) * 65
                dmaS(Mz[0:64, mb:mb + 64], m0_in[h * 64:(h + 1) * 64, :])
                dmaS(Mz[0:64, mb + 64:mb + 65],
                     z0_in[0:1, h * 64:(h + 1) * 64].rearrange("o (k u) -> (o k) u", u=1))
        vmzl = qS.n
        P.wait("vector", qS, vmzl)
        v(lambda e: e.tensor_copy(Mx_bf[:], Mz[:]))
        v(lambda e: e.memset(abf[64:65, SFB:SFB + 8192], 1.0))
        vmark()
        P.wait("gpsimd", qG, aw_done)
        barrier()

        hctx = {"tpv": 0, "vrz": 0, "recip": 0, "vmx": 0, "tnum": 0, "rowk": 0,
                "exp_s": {}, "mask_v": {}}

        def sfa_view(s):
            return abf[0:64, SFA:SFA + 8192].rearrange(
                "p (h q) -> p h q", q=512)[:, :, s * 32:s * 32 + 32]

        def sfb_view(s):
            return abf[0:64, SFB:SFB + 8192].rearrange(
                "p (h q) -> p h q", q=512)[:, :, s * 32:s * 32 + 32]

        def seg_attention(b, seg):
            t0 = b * 16 + seg * 4
            P.wait("vector", st, st.n)
            P.wait("sync", st, st.n)
            P.wait("sync", sv, sv.n)
            s1m = {}
            s1v = {}
            for tt in range(4):
                t = t0 + tt
                buf = tt % 2
                xt = slot(buf)
                if tt >= 2:
                    P.wait("sync", sv, s1v[tt - 2])
                ld = dmaS(xt, x1_d[t * 128:(t + 1) * 128, :])
                P.wait("vector", qS, ld)
                d_ = slot(2)
                v(lambda e, xt=xt, t=t, d=d_: e.tensor_scalar(
                    d, xt, m_all[:, t:t + 1], None, ALU.subtract))
                v(lambda e: e.drain())
                xb = xqt_bf[:, buf * 1024: buf * 1024 + 1024]
                if tt >= 2:
                    P.wait("vector", qS, s1m[tt - 2])
                v(lambda e, d=d_, t=t, xb=xb: e.tensor_scalar(
                    xb, d, rs_all[:, t:t + 1], None, ALU.mult))
                vd = vmark()
                s1v[tt] = vd
                P.wait("sync", sv, vd)
                dstv = abf[:, XN:XN + 4096].rearrange("p (m q) -> p m q", q=512)[
                    :, :, tt * 128:tt * 128 + 128]
                s1m[tt] = dmaS(dstv, xb, transpose=True)
            P.wait("tensor", qS, s1m[3])

            def project(wofs, sf_view_fn, scale):
                evac = {}
                td = 0
                for hc in range(DC):
                    pb = (hc % 2) * 512
                    if hc >= 2:
                        P.wait("tensor", sv, evac[hc - 2])
                    for dc in range(DC):
                        fn = (lambda e, hc=hc, dc=dc, pb=pb, wofs=wofs:
                              e.matmul(ps_f[:, pb:pb + 512],
                                       wbig(wofs + dc * 1024 + hc * 128, 128),
                                       abf[:, XN + dc * 512: XN + dc * 512 + 512],
                                       start=(dc == 0), stop=(dc == DC - 1),
                                       skip_group_check=True))
                        if dc == DC - 1:
                            td = tmm(fn)
                        else:
                            top(fn)
                    P.wait("vector", st, td)
                    for sub in range(2):
                        s = 2 * hc + sub
                        src = ps_f[sub * 64:sub * 64 + 64, pb:pb + 512].rearrange(
                            "p (h t) -> p h t", t=32)
                        dst = sf_view_fn(s)
                        if scale is None:
                            v(lambda e, dst=dst, src=src: e.tensor_copy(dst, src))
                        else:
                            v(lambda e, dst=dst, src=src, sc=scale: e.tensor_scalar(
                                dst, src, sc, None, ALU.mult))
                    evac[hc] = vmark()
                return evac[DC - 1]

            v_done = project(16384, sfa_view, None)

            P.wait("sync", sv, v_done)
            svv_dma = dmaS(
                abf[:, SVV:SVV + 4096].rearrange("p (m r) -> p m r", r=64),
                abf[0:64, SFA:SFA + 8192], transpose=True)

            k_done = project(8192, sfb_view, 0.125)
            P.wait("vector", qS, svv_dma)
            q_done = project(0, sfa_view, None)

            P.wait("sync", sv, k_done)
            svk_dma = dmaS(
                abf[:, SVK:SVK + 4096].rearrange("p (m r) -> p m r", r=64),
                abf[0:64, SFB:SFB + 8192], transpose=True)
            P.wait("vector", qS, svk_dma)
            kmarks = {}
            for c4 in range(4):
                sseg = abf[:, SVK + c4 * 1024: SVK + c4 * 1024 + 1024]
                mnb = af[:, 4096:5120]
                rlb = af[:, 5120:6144]
                v(lambda e, sseg=sseg, mnb=mnb: e.tensor_scalar(
                    mnb, sseg, 0.0, 8.0, ALU.min, ALU.mult))
                vm1 = vmark()
                P.wait("scalar", sv, vm1)
                s_(lambda e, mnb=mnb: e.activation(mnb, mnb, ACT.Exp))
                sm1 = smark()
                v(lambda e, sseg=sseg, rlb=rlb: e.tensor_scalar(
                    rlb, sseg, 0.0, 8.0, ALU.max, ALU.mult))
                v(lambda e: e.drain())
                P.wait("vector", ss, sm1)
                v(lambda e, sseg=sseg, mnb=mnb, rlb=rlb: e.tensor_tensor(
                    sseg, mnb, rlb, ALU.add))
                kmarks[c4] = vmark()
            P.wait("tensor", qS, qS.n)

            for h in range(H):
                mb = (b * 16 + h) * 65
                Qslab = abf[0:64, SFA + h * 512: SFA + (h + 1) * 512]
                mn = af[0:64, 4096:4608]
                rl = af[0:64, 4608:5120]
                P.wait("vector", st, hctx["tnum"])
                v(lambda e, Q=Qslab, mn=mn: e.tensor_scalar(mn, Q, 0.0, None, ALU.min))
                vq1 = vmark()
                P.wait("scalar", sv, vq1)
                s_(lambda e, mn=mn: e.activation(mn, mn, ACT.Exp))
                sq1 = smark()
                v(lambda e, Q=Qslab, rl=rl: e.tensor_scalar(rl, Q, 0.0, None, ALU.max))
                v(lambda e: e.drain())
                P.wait("vector", ss, sq1)
                v(lambda e, mn=mn, rl=rl: e.tensor_tensor(
                    abf[0:64, SQS:SQS + 512], mn, rl, ALU.add))
                sqs_mark = vmark()

                Kslab = abf[0:64, SFB + h * 512: SFB + (h + 1) * 512]
                P.wait("vector", st, hctx["rowk"])
                v(lambda e, Q=Qslab: e.tensor_tensor(q2b[0:64, :], Q, Q, ALU.mult))
                v(lambda e, K=Kslab: e.tensor_tensor(q2b[64:128, :], K, K, ALU.mult))
                vq2 = vmark()
                P.wait("tensor", sv, vq2)
                P.wait("tensor", sv, hctx["vrz"])
                tmm(lambda e: e.matmul(ps_f[0:1, 1024:1536], ones_bf[0:64, 0:1],
                                       q2b[0:64, :], start=True, stop=True,
                                       skip_group_check=True))
                t2 = tmm(lambda e: e.matmul(ps_f[0:1, 1536:2048], ones_bf[64:128, 0:1],
                                            q2b[64:128, :], start=True, stop=True,
                                            skip_group_check=True))
                hctx["rowk"] = t2
                P.wait("vector", st, t2)
                v(lambda e: e.reduce_max(srow[0:1, 6:7], ps_f[0:1, 1536:2048], AX))
                v(lambda e: e.drain())
                v(lambda e: e.tensor_scalar(srow[0:1, 7:8], srow[0:1, 6:7],
                                            -4.0, None, ALU.mult))
                v(lambda e: e.drain())
                v(lambda e: e.tensor_scalar(negc, ps_f[0:1, 1024:1536],
                                            -0.0625, srow[0:1, 7:8], ALU.mult, ALU.add))
                v(lambda e: e.drain())
                v(lambda e, h=h: e.tensor_copy(abf[64:65, SFA + h * 512: SFA + h * 512 + 512],
                                               negc))
                vke = vmark()
                P.wait("tensor", sv, vke)
                P.wait("tensor", sv, hctx["recip"])

                for jc in range(4):
                    pb = (jc % 2) * 512
                    pm = hctx["exp_s"].get(jc % 2)
                    if pm:
                        P.wait("tensor", ss, pm)
                    td = tmm(lambda e, jc=jc, pb=pb, h=h: e.matmul(
                        ps_f[:, pb:pb + 512],
                        abf[0:65, SFB + h * 512 + jc * 128: SFB + h * 512 + jc * 128 + 128],
                        abf[0:65, SFA + h * 512: SFA + h * 512 + 512],
                        start=True, stop=True, skip_group_check=True))
                    P.wait("scalar", st, td)
                    P.wait("scalar", st, hctx["tpv"])
                    pmv = hctx["mask_v"].get(jc)
                    if pmv:
                        P.wait("scalar", sv, pmv)
                    s_(lambda e, jc=jc, pb=pb: e.activation(
                        abf[:, PBUF + jc * 512: PBUF + jc * 512 + 512],
                        ps_f[:, pb:pb + 512], ACT.Exp))
                    hctx["exp_s"][jc % 2] = smark()
                    P.wait("vector", ss, hctx["exp_s"][jc % 2])
                    v(lambda e, jc=jc: e.tensor_tensor(
                        abf[:, PBUF + jc * 512: PBUF + jc * 512 + 512],
                        abf[:, PBUF + jc * 512: PBUF + jc * 512 + 512],
                        masks[:, jc * 512:(jc + 1) * 512], ALU.mult))
                    hctx["mask_v"][jc] = vmark()
                vpm = hctx["mask_v"][3]
                P.wait("tensor", sv, vpm)

                for jc in range(4):
                    fn = (lambda e, jc=jc: e.matmul(
                        ps_f[0:1, 1024:1536], ones_bf[:],
                        abf[:, PBUF + jc * 512: PBUF + jc * 512 + 512],
                        start=(jc == 0), stop=(jc == 3), skip_group_check=True))
                    if jc == 3:
                        tz = tmm(fn)
                    else:
                        top(fn)
                P.wait("tensor", sv, sqs_mark)
                P.wait("tensor", sv, hctx["vmx"])
                tnum = tmm(lambda e, mb=mb: e.matmul(
                    ps_f[0:65, 2048:2560], Mx_bf[0:64, mb:mb + 65],
                    abf[0:64, SQS:SQS + 512],
                    start=True, stop=True, skip_group_check=True))
                hctx["tnum"] = tnum
                for jc in range(4):
                    fn = (lambda e, jc=jc, h=h: e.matmul(
                        ps_f[0:64, 2560:3072],
                        abf[:, SVV + h * 256 + jc * 64: SVV + h * 256 + jc * 64 + 64],
                        abf[:, PBUF + jc * 512: PBUF + jc * 512 + 512],
                        start=(jc == 0), stop=(jc == 3), skip_group_check=True))
                    if jc == 3:
                        tpv = tmm(fn)
                    else:
                        top(fn)
                hctx["tpv"] = tpv
                P.wait("vector", st, tnum)
                P.wait("vector", st, tz)
                v(lambda e: e.tensor_copy(rd_row, ps_f[64:65, 2048:2560]))
                v(lambda e: e.tensor_copy(rz_row, ps_f[0:1, 1024:1536]))
                vrz = vmark()
                hctx["vrz"] = vrz
                P.wait("tensor", sv, vrz)
                tbc1 = tmm(lambda e: e.matmul(ps_f[0:64, 0:512], onesrow_f[0:1, 0:64],
                                              rd_row, start=True, stop=True,
                                              skip_group_check=True))
                tbc2 = tmm(lambda e: e.matmul(ps_f[0:64, 512:1024], onesrow_f[0:1, 0:64],
                                              rz_row, start=True, stop=True,
                                              skip_group_check=True))
                P.wait("vector", st, tbc2)
                v(lambda e: e.reciprocal(io512[0:64, :], ps_f[0:64, 0:512]))
                v(lambda e: e.drain())
                v(lambda e: e.reciprocal(af[0:64, 5120:5632], ps_f[0:64, 512:1024]))
                vrec = vmark()
                hctx["recip"] = vrec
                P.wait("vector", st, tpv)
                v(lambda e: e.tensor_tensor(cseg[0:64, 0:512], ps_f[0:64, 2048:2560],
                                            io512[0:64, :], ALU.mult))
                v(lambda e: e.tensor_tensor(cseg[0:64, 512:1024], ps_f[0:64, 2560:3072],
                                            af[0:64, 5120:5632], ALU.mult))
                v(lambda e: e.drain())
                v(lambda e: e.tensor_tensor(cseg[0:64, 1024:1536], cseg[0:64, 0:512],
                                            cseg[0:64, 512:1024], ALU.subtract))
                v(lambda e: e.drain())
                v(lambda e, h=h: e.scalar_tensor_tensor(
                    abf[0:64, SFA + h * 512: SFA + h * 512 + 512],
                    cseg[0:64, 1024:1536], gate_c[0:64, h:h + 1],
                    cseg[0:64, 512:1024], ALU.mult, ALU.add))
                vatt = vmark()
                P.wait("tensor", sv, vatt)

                for jc in range(4):
                    fn = (lambda e, jc=jc, h=h: e.matmul(
                        ps_f[0:64, 2048:2112],
                        abf[:, SVK + h * 256 + jc * 64: SVK + h * 256 + jc * 64 + 64],
                        abf[:, SVV + h * 256 + jc * 64: SVV + h * 256 + jc * 64 + 64],
                        start=(jc == 0), stop=(jc == 3), skip_group_check=True))
                    top(fn)
                for jc in range(4):
                    fn2 = (lambda e, jc=jc, h=h: e.matmul(
                        ps_f[0:64, 2560:2561],
                        abf[:, SVK + h * 256 + jc * 64: SVK + h * 256 + jc * 64 + 64],
                        ones_bf[:], start=(jc == 0), stop=(jc == 3),
                        skip_group_check=True))
                    if jc == 3:
                        tupd = tmm(fn2)
                    else:
                        top(fn2)
                P.wait("vector", st, tupd)
                v(lambda e, mb=mb: e.tensor_tensor(
                    Mz[0:64, mb:mb + 64], Mz[0:64, mb:mb + 64],
                    ps_f[0:64, 2048:2112], ALU.add))
                v(lambda e, mb=mb: e.tensor_tensor(
                    Mz[0:64, mb + 64:mb + 65], Mz[0:64, mb + 64:mb + 65],
                    ps_f[0:64, 2560:2561], ALU.add))
                v(lambda e: e.drain())
                v(lambda e, mb=mb: e.tensor_copy(
                    Mx_bf[0:64, mb:mb + 65], Mz[0:64, mb:mb + 65]))
                hctx["vmx"] = vmark()

            for s in range(16):
                src = sfa_view(s)
                dst = abf[(s % 2) * 64:(s % 2) * 64 + 64,
                          ATT + (s // 2) * 512: ATT + (s // 2) * 512 + 512].rearrange(
                    "p (h t) -> p h t", t=32)
                v(lambda e, dst=dst, src=src: e.tensor_copy(dst, src))
            shuf_done = vmark()
            P.wait("tensor", sv, shuf_done)

            wo_ev = []
            wost = []
            for tt in range(4):
                t = t0 + tt
                pb0 = (tt % 2) * 1024
                if len(wo_ev) >= 2:
                    P.wait("tensor", sv, wo_ev[-2])
                for hc in range(DC):
                    first, last = hc == 0, hc == DC - 1
                    for half in range(2):
                        fn = (lambda e, hc=hc, tt=tt, half=half, first=first, last=last, pb0=pb0:
                              e.matmul(ps_f[:, pb0 + half * 512: pb0 + half * 512 + 512],
                                       abf[:, ATT + hc * 512 + tt * 128: ATT + hc * 512 + tt * 128 + 128],
                                       wbig(24576 + hc * 1024 + half * 512, 512),
                                       start=first, stop=last, skip_group_check=True))
                        if last and half == 1:
                            td = tmm(fn)
                        else:
                            top(fn)
                xt = slot(8 + tt % 2)
                if len(wo_ev) >= 2:
                    P.wait("sync", sv, wo_ev[-2])
                ld = dmaS(xt, x1_d[t * 128:(t + 1) * 128, :])
                P.wait("vector", qS, ld)
                P.wait("vector", st, td)
                ot = slot(10 + tt % 2)
                if len(wo_ev) >= 2:
                    P.wait("vector", qG, wost[-2])
                v(lambda e, pb0=pb0, xt=xt, ot=ot: e.tensor_tensor(
                    ot, ps_f[:, pb0:pb0 + 1024], xt, ALU.add))
                vd = vmark()
                wo_ev.append(vd)
                P.wait("gpsimd", sv, vd)
                wost.append(dmaG(x2_d[t * 128:(t + 1) * 128, :], ot))

        for seg in range(4):
            for b in range(2):
                seg_attention(b, seg)
        barrier()

        ffn(x2_d, out_d, 4)

        P.wait("gpsimd", qG, qG.n)
        P.wait("sync", qS, qS.n)

        with nc.Block() as block:
            for eng_name in P.ENGINES:
                fns = P.q[eng_name]
                getattr(block, eng_name)(lambda e, fns=fns: [f(e) for f in fns])

    return nc


def kernel(L0_attn_x, L0_attn_Wq, L0_attn_Wk, L0_attn_Wv, L0_attn_Wo,
           L0_attn_betas, L0_attn_memory0, L0_attn_z0,
           L0_ffn_w_gate, L0_ffn_w_up, L0_ffn_w_down):
    x = np.ascontiguousarray(np.asarray(L0_attn_x, dtype=np.float32))
    B, S, Dm = x.shape
    nc = build_program()
    common = {
        "wq_in": np.ascontiguousarray(np.asarray(L0_attn_Wq, np.float32)),
        "wk_in": np.ascontiguousarray(np.asarray(L0_attn_Wk, np.float32)),
        "wv_in": np.ascontiguousarray(np.asarray(L0_attn_Wv, np.float32)),
        "wo_in": np.ascontiguousarray(np.asarray(L0_attn_Wo, np.float32)),
        "betas_in": np.ascontiguousarray(np.asarray(L0_attn_betas, np.float32).reshape(1, 1024)),
        "m0_in": np.ascontiguousarray(np.asarray(L0_attn_memory0, np.float32).reshape(1024, 64)),
        "z0_in": np.ascontiguousarray(np.asarray(L0_attn_z0, np.float32).reshape(1, 1024)),
        "wg_in": np.ascontiguousarray(np.asarray(L0_ffn_w_gate, np.float32)),
        "wu_in": np.ascontiguousarray(np.asarray(L0_ffn_w_up, np.float32)),
        "wd_in": np.ascontiguousarray(np.asarray(L0_ffn_w_down, np.float32)),
    }
    in_maps = []
    for c in range(NCORES):
        m = dict(common)
        m["x_in"] = np.ascontiguousarray(x[2 * c:2 * c + 2].reshape(R, Dm))
        in_maps.append(m)
    res = run_bass_kernel_spmd(nc, in_maps, core_ids=list(range(NCORES)))
    out = np.empty((B, S, Dm), np.float32)
    for c in range(NCORES):
        out[2 * c:2 * c + 2] = np.asarray(res.results[c]["out"]).reshape(2, S, Dm)
    return out
